# version 17
# speedup vs baseline: 1.5615x; 1.0183x over previous
import numpy as np
import ml_dtypes
import concourse.bass as bass
import concourse.mybir as mybir
from concourse.bass_utils import run_bass_kernel_spmd

F32 = mybir.dt.float32
BF16 = mybir.dt.bfloat16
I32 = mybir.dt.int32
AF = mybir.ActivationFunctionType
ALU = mybir.AluOpType
AX = mybir.AxisListType

D = 2048
S = 4096
NB = 4
H = 16
KVH = 4
DH = 128
DF = 8192
TOK = 2048
NSLOT = 4
KTS = [8, 16, 24, 32]

C_SCALE = float(DH) ** -0.5


class Tok:
    __slots__ = ("sem", "val")

    def __init__(self, sem, val):
        self.sem, self.val = sem, val


class Eng:

    def __init__(self, eng, sem, name):
        self.e, self.sem, self.name = eng, sem, name
        self.n = 0
        self.seen = {}

    def wait(self, tok):
        if tok is None:
            return
        if isinstance(tok, (list, tuple)):
            for t in tok:
                self.wait(t)
            return
        if isinstance(tok, QTok):
            tok = tok.resolve()
        if tok.sem is self.sem:
            if tok.val <= self.n:
                return
        if self.seen.get(id(tok.sem), -1) >= tok.val:
            return
        self.e.wait_ge(tok.sem, tok.val)
        self.seen[id(tok.sem)] = tok.val

    chain_default = True
    inc_default = True

    def op(self, fn, *a, deps=None, chain=None, inc=None, **k):
        if chain is None:
            chain = self.chain_default
        if inc is None:
            inc = self.inc_default
        if deps:
            self.wait(deps)
        if chain and self.n:
            self.e.wait_ge(self.sem, self.n)
        r = fn(*a, **k)
        if inc:
            r.then_inc(self.sem, 1)
            self.n += 1
            return Tok(self.sem, self.n)
        return None

    def tok(self):
        return Tok(self.sem, self.n)


class TEng(Eng):

    chain_default = False
    inc_default = False

    def op(self, fn, *a, deps=None, chain=None, inc=None, **k):
        if inc is None:
            inc = self.inc_default
        if inc:
            chain = True
        return super().op(fn, *a, deps=deps, chain=chain, inc=inc, **k)


class QTok:

    __slots__ = ("q",)

    def __init__(self, q):
        self.q = q

    def resolve(self):
        return Tok(self.q.sem, self.q.n * 16)


class Q:

    def __init__(self, eng, sem, name):
        self.e, self.sem, self.name = eng, sem, name
        self.n = 0
        self.seen = {}

    def _wait1(self, tok):
        if isinstance(tok, QTok):
            if tok.q is self:
                return
            tok = tok.resolve()
        if tok.sem is self.sem:
            return
        if self.seen.get(id(tok.sem), -1) >= tok.val:
            return
        self.e.wait_ge(tok.sem, tok.val)
        self.seen[id(tok.sem)] = tok.val

    def wait(self, tok):
        if tok is None:
            return
        if isinstance(tok, (list, tuple)):
            for t in tok:
                if t is not None:
                    self._wait1(t)
            return
        self._wait1(tok)

    def dma(self, out, in_, deps=None):
        if deps:
            self.wait(deps)
        self.e.dma_start(out, in_).then_inc(self.sem, 16)
        self.n += 1
        return QTok(self)

    def tok(self):
        return QTok(self)


def mxq_chain(V, SE, G, x_ap, nblk, buf, out_bf16):
    am, ah, ei, sc, iv = (buf[n] for n in ("am", "ah", "ei", "sc", "iv"))
    yq, q0i = (buf[n] for n in ("yq", "q0i"))
    emask = buf["emask"]
    x3 = x_ap.rearrange("p (b c) -> p b c", c=32)
    t_am = V.op(V.e.tensor_reduce, am[:, :nblk], x3, AX.X, ALU.max,
                apply_absolute_value=True, deps=buf.get("xdep"))
    V.op(V.e.tensor_scalar, ah[:, :nblk], am[:, :nblk], 2.0 ** -64, 0.0, ALU.max,
         ALU.bypass, deps=t_am)
    V.op(V.e.tensor_single_scalar, ei[:, :nblk], ah[:, :nblk].bitcast(I32), emask[:],
         ALU.bitwise_and)
    pe2v = ei[:, :nblk].bitcast(F32)
    V.op(V.e.tensor_scalar, sc[:, :nblk], pe2v, 2.0 ** -6, 0.0, ALU.mult, ALU.bypass)
    V.op(V.e.reciprocal, iv[:, :nblk], pe2v)
    t_iv = V.op(V.e.tensor_scalar, iv[:, :nblk], iv[:, :nblk], 64.0, 0.0, ALU.mult, ALU.bypass)
    n = nblk * 32
    if buf.get("cast_se"):
        nh = nblk // 2
        n2 = n // 2
        yeng = G if buf.get("yg") else V
        t_cast = []
        for hh in range(2):
            bs = slice(hh * nh, (hh + 1) * nh)
            iv3 = iv[:, bs].rearrange("p (b o) -> p b o", o=1).broadcast_to(
                [128, nh, 32])
            y3 = yq[:, hh * n2:(hh + 1) * n2].rearrange("p (b c) -> p b c", c=32)
            x3h = x_ap[:, hh * n2:(hh + 1) * n2].rearrange("p (b c) -> p b c", c=32)
            t_y = yeng.op(yeng.e.tensor_tensor, y3, x3h, iv3, ALU.mult,
                          deps=t_iv if hh == 0 or yeng is not V else None)
            t_cast.append(SE.op(SE.e.copy, q0i[:, hh * n2:(hh + 1) * n2],
                                yq[:, hh * n2:(hh + 1) * n2],
                                deps=[t_y, buf.get("se_war")] if hh == 0
                                else t_y))
        t_last = None
        for hh in range(2):
            bs = slice(hh * nh, (hh + 1) * nh)
            V.wait(t_cast[hh])
            sc3 = sc[:, bs].rearrange("p (b o) -> p b o", o=1).broadcast_to(
                [128, nh, 32])
            q3 = q0i[:, hh * n2:(hh + 1) * n2].rearrange("p (b c) -> p b c", c=32)
            o3 = out_bf16[:, hh * n2:(hh + 1) * n2].rearrange(
                "p (b c) -> p b c", c=32)
            t_last = V.op(V.e.scalar_tensor_tensor, o3, q3, 127.0, sc3,
                          ALU.min, ALU.mult)
        return t_last
    iv3 = iv[:, :nblk].rearrange("p (b o) -> p b o", o=1).broadcast_to([128, nblk, 32])
    y3 = yq[:, :n].rearrange("p (b c) -> p b c", c=32)
    V.op(V.e.tensor_tensor, y3, x3, iv3, ALU.mult, deps=t_iv)
    V.op(V.e.tensor_copy, q0i[:, :n], yq[:, :n])
    sc3 = sc[:, :nblk].rearrange("p (b o) -> p b o", o=1).broadcast_to([128, nblk, 32])
    q3 = q0i[:, :n].rearrange("p (b c) -> p b c", c=32)
    o3 = out_bf16.rearrange("p (b c) -> p b c", c=32)
    return V.op(V.e.scalar_tensor_tensor, o3, q3, 127.0, sc3, ALU.min, ALU.mult)


def lnstats(V, SE, bn6, out_mu, out_rs, x_ap, cpool, eps_hi_col, eps_lo_col, tmp1):
    for c in range(4):
        V.op(V.e.bn_stats, bn6[:, 6 * c:6 * c + 6], x_ap[:, 512 * c:512 * (c + 1)])
    V.op(V.e.bn_aggr, bn6[:, 0:2], bn6[:])
    V.op(V.e.tensor_copy, out_mu[:], bn6[:, 0:1])
    V.op(V.e.tensor_scalar, tmp1[:], bn6[:, 1:2], eps_hi_col, eps_lo_col,
         ALU.add, ALU.add)
    t = V.tok()
    tl = SE.op(SE.e.activation, tmp1[:], tmp1[:], AF.Ln, deps=t)
    return SE.op(SE.e.activation, out_rs[:], tmp1[:], AF.Exp,
                 scale=cpool[:, 6:7])


def lnstats(V, SE, bn6, bnm, out_mu, out_rs, x_ap, cpool, tmp1):
    for c in range(4):
        V.op(V.e.bn_stats, bn6[:, 6 * c:6 * c + 6], x_ap[:, 512 * c:512 * (c + 1)])
    V.op(V.e.bn_aggr, bnm[:], bn6[:])
    V.op(V.e.tensor_copy, out_mu[:], bnm[:, 0:1])
    V.op(V.e.tensor_scalar, tmp1[:], bnm[:, 1:2], cpool[:, 0:1], cpool[:, 1:2],
         ALU.add, ALU.add)
    t = V.tok()
    SE.op(SE.e.activation, tmp1[:], tmp1[:], AF.Ln, deps=t)
    return SE.op(SE.e.activation, out_rs[:], tmp1[:], AF.Exp, scale=cpool[:, 6:7])


def build(upto="p3", taps=()):
    import contextlib
    nc = bass.Bass("TRN2", target_bir_lowering=False)
    dt = nc.dram_tensor
    x_seq = dt("x_seq", [S, D], F32, kind="ExternalInput").ap()
    x_own = dt("x_own", [TOK, D], F32, kind="ExternalInput").ap()
    qidx = dt("qidx", [1, TOK], F32, kind="ExternalInput").ap()
    Wq = dt("Wq", [D, D], F32, kind="ExternalInput").ap()
    Wk = dt("Wk", [D, 512], F32, kind="ExternalInput").ap()
    Wv = dt("Wv", [D, 512], F32, kind="ExternalInput").ap()
    Wo = dt("Wo", [D, D], F32, kind="ExternalInput").ap()
    W1 = dt("W1", [DF, D], F32, kind="ExternalInput").ap()
    W2 = dt("W2", [D, DF], F32, kind="ExternalInput").ap()
    qg = dt("qg", [1, 128], F32, kind="ExternalInput").ap()
    kg = dt("kg", [1, 128], F32, kind="ExternalInput").ap()
    lnw = dt("lnw", [1, D], F32, kind="ExternalInput").ap()
    lnb = dt("lnb", [1, D], F32, kind="ExternalInput").ap()
    b1r = dt("b1r", [1, DF], F32, kind="ExternalInput").ap()
    b2r = dt("b2r", [1, D], F32, kind="ExternalInput").ap()
    onescol = dt("onescol", [128, 1], BF16, kind="ExternalInput").ap()
    onesq_in = dt("onesq_in", [128, 128], BF16, kind="ExternalInput").ap()
    onesrowf = dt("onesrowf", [1, 128], F32, kind="ExternalInput").ap()
    meancol = dt("meancol", [128, 1], F32, kind="ExternalInput").ap()
    idn_in = dt("idn_in", [128, 128], BF16, kind="ExternalInput").ap()
    emask_in = dt("emask_in", [128, 1], I32, kind="ExternalInput").ap()
    cpool_in = dt("cpool_in", [128, 8], F32, kind="ExternalInput").ap()
    out_own = dt("out_own", [TOK, D], F32, kind="ExternalOutput").ap()
    wq_s = dt("wq_s", [D, D], BF16).ap()
    wo_s = dt("wo_s", [D, D], BF16).ap()
    w1tile_s = dt("w1tile_s", [16 * 16 * 128, 512], BF16).ap()
    w2t_s = dt("w2t_s", [DF, D], BF16).ap()
    x1_s = dt("x1_s", [TOK, D], F32).ap()
    tapd = {}
    if "w1t" in taps:
        tapd["t_w1t"] = dt("t_w1t", [D, DF], BF16, kind="ExternalOutput").ap()
        tapd["t_w2t"] = dt("t_w2t", [DF, D], BF16, kind="ExternalOutput").ap()
        tapd["t_wq"] = dt("t_wq", [D, D], BF16, kind="ExternalOutput").ap()
    if "kv" in taps:
        tapd["t_kfm"] = dt("t_kfm", [128, KVH * S], BF16, kind="ExternalOutput").ap()
        tapd["t_vtm"] = dt("t_vtm", [128, 32 * 512], BF16, kind="ExternalOutput").ap()
    if "x1" in taps:
        tapd["t_x1"] = dt("t_x1", [TOK, D], F32, kind="ExternalOutput").ap()
    if "ffn" in taps:
        tapd["t_h1q"] = dt("t_h1q", [512, D], BF16, kind="ExternalOutput").ap()
        tapd["t_gq"] = dt("t_gq", [512, D], BF16, kind="ExternalOutput").ap()

    st = contextlib.ExitStack()
    with st:
        E = st.enter_context
        sT = E(nc.semaphore("sT"))
        sV = E(nc.semaphore("sV"))
        sS = E(nc.semaphore("sS"))
        sG = E(nc.semaphore("sG"))
        blk = E(nc.Block())

        T = TEng(nc.tensor, sT, "T")
        V = Eng(nc.vector, sV, "V")
        SE = Eng(nc.scalar, sS, "S")
        G = Eng(nc.gpsimd, sG, "G")

        all_qs = []

        def mkq(name, eng=None):
            sem = E(nc.semaphore("d_" + name))
            q = Q(eng if eng is not None else nc.sync, sem, name)
            all_qs.append(q)
            return q

        qC = mkq("C")
        qSW = mkq("SW", nc.gpsimd)
        qWg_hold = [None]

        def barrier():
            T.op(T.e.nop, inc=True)
            toks = [T.tok(), V.tok(), SE.tok(), G.tok()] + [q.tok() for q in all_qs]
            for e in (T, V, SE, G):
                e.wait(toks)
            for q in all_qs:
                q.wait(toks)

        cpool = E(nc.sbuf_tensor("cpool", [128, 8], F32))
        emask = E(nc.sbuf_tensor("emask", [128, 1], I32))
        idn = E(nc.sbuf_tensor("idn", [128, 128], BF16))
        onec = E(nc.sbuf_tensor("onec", [128, 1], BF16))
        onesq = E(nc.sbuf_tensor("onesq", [128, 128], BF16))
        oner = E(nc.sbuf_tensor("oner", [1, 128], F32))
        meanc = E(nc.sbuf_tensor("meanc", [128, 1], F32))
        qgr = E(nc.sbuf_tensor("qgr", [1, 128], F32))
        kgr = E(nc.sbuf_tensor("kgr", [1, 128], F32))
        qidxs = E(nc.sbuf_tensor("qidxs", [1, TOK], F32))
        kidxT = E(nc.sbuf_tensor("kidxT", [128, 32], F32))
        t_c = {}
        for nm, dst2, src2 in [("cpool", cpool, cpool_in), ("emask", emask, emask_in),
                               ("idn", idn, idn_in), ("onec", onec, onescol),
                               ("onesq", onesq, onesq_in),
                               ("oner", oner, onesrowf), ("meanc", meanc, meancol),
                               ("qgr", qgr, qg), ("kgr", kgr, kg),
                               ("qidx", qidxs, qidx)]:
            t_c[nm] = qC.dma(dst2[:], src2)
        G.op(G.e.iota, kidxT[:], [[128, 32]], base=0, channel_multiplier=1,
             allow_small_or_imprecise_dtypes=True)

        with contextlib.ExitStack() as p0:
            E0 = p0.enter_context
            wf = E0(nc.sbuf_tensor("wf", [128, 2 * DF], F32))
            wqo = E0(nc.sbuf_tensor("wqo", [128, DF], BF16))
            stg = E0(nc.sbuf_tensor("stg", [128, 16384], BF16))
            qb = {}
            for nm in ("am", "ah", "sc", "iv"):
                qb[nm] = E0(nc.sbuf_tensor("q0_" + nm, [128, 64], F32))
            qb["ei"] = E0(nc.sbuf_tensor("q0_ei", [128, 64], I32))
            qb["yq"] = E0(nc.sbuf_tensor("q0_yq", [128, 2048], F32))
            qb["q0i"] = E0(nc.sbuf_tensor("q0_q0i", [128, 2048], I32))
            qb["emask"] = emask
            qb["use_g"] = False
            qb["cast_se"] = True
            ptr0 = [E0(nc.psum_tensor("ptr0a", [128, 512], BF16)),
                    E0(nc.psum_tensor("ptr0b", [128, 512], BF16))]

            V.wait(t_c["emask"])
            T.wait(t_c["idn"])
            qW = [mkq("W0"), mkq("W1ld")]
            qWg_hold[0] = mkq("Wg", nc.gpsimd)

            tload = [None, None]
            tstore = [None, None]
            tcast = [None, None]
            for (w_in, w_out) in [(Wq, wq_s), (Wo, wo_s)]:
                for kc in range(16):
                    par = kc % 2
                    half = wf[:, par * DF:par * DF + D]
                    tload[par] = qW[par].dma(half, w_in[kc * 128:(kc + 1) * 128, :],
                                             deps=tcast[par])
                    oslice = wqo[:, par * D:(par + 1) * D]
                    tcast[par] = SE.op(SE.e.copy, oslice, half,
                                       deps=[tload[par], tstore[par]])
                    tstore[par] = qSW.dma(w_out[kc * 128:(kc + 1) * 128, :], oslice,
                                          deps=tcast[par])
            barrier()
            if upto == "p0cast":
                if "t_w1t" in tapd:
                    for r in range(16):
                        qC.dma(tapd["t_wq"][r * 128:(r + 1) * 128, :],
                               wq_s[r * 128:(r + 1) * 128, :])
                fin = qC.tok().resolve()
                nc.sync.wait_ge(fin.sem, fin.val)
                return nc

            tload = [None, None]
            tq = [None, None]
            tewr = [None] * 4
            tev_prev = [None] * 4
            for oc in range(64):
                par = oc % 2
                half = wf[:, par * DF:par * DF + D]
                tload[par] = (qW[0] if par == 0 else qWg_hold[0]).dma(
                    half, W1[oc * 128:(oc + 1) * 128, :], deps=tq[par])
                V.wait(tload[par])
                qb["xdep"] = [tload[par], V.tok()]
                qb["se_war"] = V.tok()
                w1q = wqo[:, par * D:(par + 1) * D]
                tq[par] = mxq_chain(V, SE, G, half, 64, qb, w1q)
                qb.pop("xdep", None)
                om = oc % 4
                tev = []
                for tb in range(4):
                    sl = tb % 2
                    dps = [tq[par]]
                    if tb >= 2:
                        dps.append(tev[tb - 2])
                    elif tev_prev[2 + sl] is not None:
                        dps.append(tev_prev[2 + sl])
                    ttr = None
                    for u in range(4):
                        ti = tb * 4 + u
                        ttr = T.op(T.e.transpose, ptr0[sl][:, u * 128:(u + 1) * 128],
                                   w1q[:, ti * 128:(ti + 1) * 128], idn[:],
                                   deps=dps if u == 0 else None, inc=(u == 3))
                    evdeps = [ttr]
                    if om == 0 and tewr[tb] is not None:
                        evdeps.append(tewr[tb])
                    st4 = stg[:].rearrange("p (t o) -> p t o", o=512).rearrange(
                        "p t (w c) -> p t w c", c=128)
                    tev.append(SE.op(
                        SE.e.copy,
                        st4[:, tb * 4:(tb + 1) * 4, om, :],
                        ptr0[sl][:].rearrange("p (w c) -> p w c", c=128),
                        deps=evdeps))
                tev_prev = tev
                if om == 3:
                    ob_ = oc // 4
                    for tb in range(4):
                        r0 = (ob_ * 16 + tb * 4) * 128
                        tewr[tb] = qSW.dma(
                            w1tile_s[r0:r0 + 512, :].rearrange(
                                "(t p) j -> p t j", p=128),
                            stg[:, tb * 2048:(tb + 1) * 2048].rearrange(
                                "p (t j) -> p t j", j=512),
                            deps=tev[tb])
            barrier()
            if upto == "p0w1":
                if "t_w1t" in tapd:
                    for r in range(16):
                        qC.dma(tapd["t_w1t"][r * 128:(r + 1) * 128, :],
                               w1t_s[r * 128:(r + 1) * 128, :])
                fin = qC.tok().resolve()
                nc.sync.wait_ge(fin.sem, fin.val)
                return nc

            tq2 = None
            tewr = [None] * 16
            tev_prev = [None] * 16
            for oc in range(16):
                tload2 = qW[0].dma(wf[:, 0:DF], W2[oc * 128:(oc + 1) * 128, :],
                                   deps=tq2)
                V.wait(tload2)
                for sub in range(4):
                    qb["xdep"] = [tload2, V.tok()]
                    qb["se_war"] = V.tok()
                    tq2 = mxq_chain(V, SEE := SE, G, wf[:, sub * 2048:(sub + 1) * 2048], 64,
                                    qb, wqo[:, sub * 2048:(sub + 1) * 2048])
                    qb.pop("xdep", None)
                om = oc % 2
                tev = []
                st4 = stg[:].rearrange("p (t o) -> p t o", o=256).rearrange(
                    "p t (w c) -> p t w c", c=128)
                for tb in range(16):
                    sl = tb % 2
                    dps = [tq2]
                    if tb >= 2:
                        dps.append(tev[tb - 2])
                    elif tev_prev[14 + sl] is not None:
                        dps.append(tev_prev[14 + sl])
                    ttr = None
                    for u in range(4):
                        ti = tb * 4 + u
                        ttr = T.op(T.e.transpose, ptr0[sl][:, u * 128:(u + 1) * 128],
                                   wqo[:, ti * 128:(ti + 1) * 128], idn[:],
                                   deps=dps if u == 0 else None, inc=(u == 3))
                    evdeps = [ttr]
                    if om == 0 and tewr[tb] is not None:
                        evdeps.append(tewr[tb])
                    tev.append(SE.op(
                        SE.e.copy,
                        st4[:, tb * 4:(tb + 1) * 4, om, :],
                        ptr0[sl][:].rearrange("p (w c) -> p w c", c=128),
                        deps=evdeps))
                tev_prev = tev
                if om == 1:
                    for tb in range(16):
                        tewr[tb] = qSW.dma(
                            w2t_s[tb * 512:(tb + 1) * 512, (oc - 1) * 128:(oc + 1) * 128]
                            .rearrange("(t p) j -> p t j", p=128),
                            stg[:, tb * 1024:(tb + 1) * 1024].rearrange(
                                "p (t j) -> p t j", j=256),
                            deps=tev[tb])
            barrier()

        if "t_w1t" in tapd:
            for r in range(16):
                qC.dma(tapd["t_w1t"][r * 128:(r + 1) * 128, :],
                       w1tile_s[r * 2048:(r + 1) * 2048, :].rearrange(
                           "(i p) j -> p (i j)", p=128))
                qC.dma(tapd["t_wq"][r * 128:(r + 1) * 128, :],
                       wq_s[r * 128:(r + 1) * 128, :])
            for r in range(64):
                qC.dma(tapd["t_w2t"][r * 128:(r + 1) * 128, :],
                       w2t_s[r * 128:(r + 1) * 128, :])
        if upto == "p0":
            fin = qC.tok().resolve()
            nc.sync.wait_ge(fin.sem, fin.val)
            return nc

        pkv = contextlib.ExitStack()
        EKV = pkv.enter_context
        kfm = EKV(nc.sbuf_tensor("kfm", [128, KVH * S], BF16))
        vtm = EKV(nc.sbuf_tensor("vtm", [128, 32 * 512], BF16))
        with contextlib.ExitStack() as p1:
            E1 = p1.enter_context
            wkb = E1(nc.sbuf_tensor("wkb", [128, 16 * 512], BF16))
            wvb = E1(nc.sbuf_tensor("wvb", [128, 16 * 512], BF16))
            wkvf = E1(nc.sbuf_tensor("wkvf", [128, 1024], F32))
            xb = E1(nc.sbuf_tensor("xb", [128, 2 * D], F32))
            lnbuf = E1(nc.sbuf_tensor("lnbuf", [128, 2 * D], BF16))
            lnT = E1(nc.sbuf_tensor("lnT", [128, 16 * 512], BF16))
            bn6 = E1(nc.sbuf_tensor("bn6", [128, 24], F32))
            bnm = E1(nc.sbuf_tensor("bnm", [128, 2], F32))
            smu = E1(nc.sbuf_tensor("smu", [128, 1], F32))
            srs = E1(nc.sbuf_tensor("srs", [128, 1], F32))
            stmp = E1(nc.sbuf_tensor("stmp", [128, 1], F32))
            k2f = E1(nc.sbuf_tensor("k2f", [128, 512], F32))
            Bsb = E1(nc.sbuf_tensor("Bsb", [128, 512], F32))
            rk1 = E1(nc.sbuf_tensor("rk1", [1, 512], F32))
            pln = [E1(nc.psum_tensor("plnA", [128, 512], BF16)),
                   E1(nc.psum_tensor("plnB", [128, 512], BF16))]
            pk = [E1(nc.psum_tensor(f"pk{i}", [128, 512], F32)) for i in range(4)]
            pms = E1(nc.psum_tensor("pms", [1, 512], F32))
            pB = E1(nc.psum_tensor("pB", [128, 512], F32))
            qX = [mkq("Xa"), mkq("Xb")]

            t_wl = [None, None]
            t_wc = [None, None]
            for i, (w_in, w_dst) in enumerate([(Wk, wkb), (Wv, wvb)]):
                for kc in range(16):
                    par = (i * 16 + kc) % 2
                    slot = wkvf[:, par * 512:(par + 1) * 512]
                    t_wl[par] = qW[par].dma(slot, w_in[kc * 128:(kc + 1) * 128, :],
                                            deps=t_wc[par])
                    t_wc[par] = SE.op(SE.e.copy, w_dst[:, kc * 512:(kc + 1) * 512],
                                      slot, deps=t_wl[par])
            t_wkv = SE.tok()
            V.wait(t_c["cpool"])
            SE.wait(t_c["cpool"])

            T.wait(t_c["meanc"])
            T.wait(t_c["kgr"])
            t_ms_prev = None
            t_lnap = [None, None]
            t_ttr_half = [None, None]
            t_evac = [None, None]
            t_lnT_free = None
            t_kfm = {}
            t_vtm = {}
            t_pk_free = [None] * 4
            t_pms_ev = None
            t_pB_ev = None
            for kvb in range(8):
                for tt in range(4):
                    ti = kvb * 4 + tt
                    par = ti % 2
                    xh = xb[:, par * D:(par + 1) * D]
                    t_x = qX[par].dma(xh, x_seq[ti * 128:(ti + 1) * 128, :],
                                      deps=t_lnap[par])
                    V.wait(t_x)
                    lnstats(V, SE, bn6, bnm, smu, srs, xh, cpool, stmp)
                    lh = lnbuf[:, par * D:(par + 1) * D]
                    t_lnap[par] = V.op(V.e.tensor_scalar, lh, xh, smu[:], srs[:],
                                       ALU.subtract, ALU.mult, deps=SE.tok())
                    lnT4 = lnT[:].rearrange("p (t o) -> p t o", o=512).rearrange(
                        "p t (w c) -> p t w c", c=128)
                    for tb in range(4):
                        sl = tb % 2
                        dps = [t_lnap[par], t_evac[sl]]
                        ttr = None
                        for u in range(4):
                            kc = tb * 4 + u
                            ttr = T.op(T.e.transpose,
                                       pln[sl][:, u * 128:(u + 1) * 128],
                                       lh[:, kc * 128:(kc + 1) * 128], idn[:],
                                       deps=dps if u == 0 else None, inc=(u == 3))
                        t_ttr_half[par] = ttr
                        w = tt * 128 // 128
                        t_evac[sl] = V.op(
                            V.e.tensor_copy,
                            lnT4[:, tb * 4:(tb + 1) * 4, tt, :],
                            pln[sl][:].rearrange("p (w c) -> p w c", c=128),
                            deps=[ttr, t_lnT_free])
                for oc in range(4):
                    tk = None
                    for kc in range(16):
                        tk = T.op(T.e.matmul, pk[oc][:],
                                  wkb[:, kc * 512 + oc * 128: kc * 512 + oc * 128 + 128],
                                  lnT[:, kc * 512:(kc + 1) * 512],
                                  start=(kc == 0), stop=(kc == 15),
                                  deps=[t_evac[0], t_evac[1], t_pk_free[oc],
                                        t_wkv],
                                  inc=(kc == 15))
                    t_sq = SE.op(SE.e.activation, k2f[:], pk[oc][:], AF.Square,
                                 deps=[tk, t_ms_prev])
                    tms = T.op(T.e.matmul, pms[:], meanc[:], k2f[:], start=True,
                               stop=True, deps=[t_sq, t_pms_ev], inc=True)
                    t_ms_prev = tms
                    V.op(V.e.tensor_scalar, rk1[:], pms[:], cpool[0:1, 2:3], 0.0,
                         ALU.add, ALU.bypass, deps=tms)
                    t_pms_ev = V.tok()
                    SE.op(SE.e.activation, rk1[:], rk1[:], AF.Ln, deps=t_pms_ev)
                    t_rk = SE.op(SE.e.activation, rk1[:], rk1[:], AF.Exp,
                                 scale=cpool[0:1, 6:7])
                    tBm = T.op(T.e.matmul, pB[:], kgr[:], rk1[:], start=True,
                               stop=True, deps=[t_rk, t_pB_ev], inc=True)
                    t_pB_ev = V.op(V.e.tensor_copy, Bsb[:], pB[:], deps=tBm)
                    t_kfm[(kvb, oc)] = V.op(
                        V.e.tensor_tensor,
                        kfm[:, oc * S + kvb * 512: oc * S + (kvb + 1) * 512],
                        pk[oc][:], Bsb[:], ALU.mult)
                    t_pk_free[oc] = t_kfm[(kvb, oc)]
                for tt2 in range(4):
                    b = tt2 % 2
                    tv = None
                    for kc in range(16):
                        tv = T.op(T.e.matmul, pk[b][:],
                                  lnT[:, kc * 512 + tt2 * 128: kc * 512 + tt2 * 128 + 128],
                                  wvb[:, kc * 512:(kc + 1) * 512],
                                  start=(kc == 0), stop=(kc == 15),
                                  deps=[t_pk_free[b]], inc=(kc == 15))
                    t_vtm[(kvb, tt2)] = SE.op(
                        SE.e.copy, vtm[:, (kvb * 4 + tt2) * 512:(kvb * 4 + tt2 + 1) * 512],
                        pk[b][:], deps=tv)
                    t_pk_free[b] = t_vtm[(kvb, tt2)]
                t_lnT_free = T.tok()
            barrier()

        if "t_kfm" in tapd:
            qC.dma(tapd["t_kfm"], kfm[:])
            qC.dma(tapd["t_vtm"], vtm[:])
        if upto == "p1":
            fin = qC.tok().resolve()
            nc.sync.wait_ge(fin.sem, fin.val)
            return nc

        lnT2 = EKV(nc.sbuf_tensor("lnT2", [128, 16 * 512], BF16))
        wqb = EKV(nc.sbuf_tensor("wqb", [128, 4 * 1024], BF16))
        q_raw = EKV(nc.sbuf_tensor("q_raw", [128, 16 * 512], F32))
        qfm = EKV(nc.sbuf_tensor("qfm", [128, 16 * 512], BF16))
        attn = EKV(nc.sbuf_tensor("attn", [128, 16 * 512], BF16))
        xb2 = EKV(nc.sbuf_tensor("xb2", [128, 2 * D], F32))
        lnb2 = EKV(nc.sbuf_tensor("lnb2", [128, D], BF16))
        Pm = EKV(nc.sbuf_tensor("Pm", [128, 2 * 1024], BF16))
        Pm2 = EKV(nc.sbuf_tensor("Pm2", [128, 2 * 1024], BF16))
        qbc = EKV(nc.sbuf_tensor("qbc", [128, 512], F32))
        rq1 = EKV(nc.sbuf_tensor("rq1", [1, 512], F32))
        rq2 = EKV(nc.sbuf_tensor("rq2", [1, 512], F32))
        Bs2 = EKV(nc.sbuf_tensor("Bs2", [128, 512], F32))
        q2f = EKV(nc.sbuf_tensor("q2f", [128, 512], F32))
        bn6b = EKV(nc.sbuf_tensor("bn6b", [128, 24], F32))
        bnmb = EKV(nc.sbuf_tensor("bnmb", [128, 2], F32))
        statsS = EKV(nc.sbuf_tensor("statsS", [128, 8], F32))
        sAc = EKV(nc.sbuf_tensor("sAc", [128, 1], F32))
        sBc = EKV(nc.sbuf_tensor("sBc", [128, 1], F32))
        smu2 = EKV(nc.sbuf_tensor("smu2", [128, 1], F32))
        srs2 = EKV(nc.sbuf_tensor("srs2", [128, 1], F32))
        stmp2 = EKV(nc.sbuf_tensor("stmp2", [128, 1], F32))
        qWO = [mkq("WOa"), mkq("WOb")]
        qX1 = mkq("X1w", nc.gpsimd)
        qBCs = [mkq("BCa"), mkq("BCb")]
        qWq4 = [qW[0], qW[1], qBCs[0], qBCs[1]]

        T.wait(t_c["onec"])
        T.wait(t_c["onesq"])
        T.wait(t_c["oner"])
        T.wait(t_c["qgr"])
        V.wait(Tok(sG, 1))
        t_qraw_free = None
        t_lnT2_free = None
        t_wqb = [None, None]
        t_qfm_free = None
        t_attn_free = None
        for s in range(NSLOT):
            with contextlib.ExitStack() as pa:
                EA = pa.enter_context
                plq = [EA(nc.psum_tensor(f"plq{s}_{i}", [128, 512], BF16))
                       for i in range(2)]
                pq = [EA(nc.psum_tensor(f"pq{s}_{i}", [128, 512], F32))
                      for i in range(6)]
                t_ev = [None, None]
                t_lnap2 = [None, None]
                for tt in range(4):
                    par = tt % 2
                    xh = xb2[:, par * D:(par + 1) * D]
                    t_x = qX[par].dma(xh, x_own[(s * 4 + tt) * 128:(s * 4 + tt + 1) * 128, :],
                                      deps=t_lnap2[par])
                    V.wait(t_x)
                    lnstats(V, SE, bn6b, bnmb, smu2, srs2, xh, cpool, stmp2)
                    V.op(V.e.tensor_copy, statsS[:, 2 * tt:2 * tt + 1], smu2[:])
                    V.op(V.e.tensor_copy, statsS[:, 2 * tt + 1:2 * tt + 2], srs2[:],
                         deps=SE.tok())
                    t_lnap2[par] = V.op(V.e.tensor_scalar, lnb2[:], xh, smu2[:],
                                        srs2[:], ALU.subtract, ALU.mult)
                    lnT24 = lnT2[:].rearrange("p (t o) -> p t o", o=512).rearrange(
                        "p t (w c) -> p t w c", c=128)
                    for tb in range(4):
                        sl = tb % 2
                        ttr = None
                        for u in range(4):
                            kc = tb * 4 + u
                            ttr = T.op(T.e.transpose,
                                       plq[sl][:, u * 128:(u + 1) * 128],
                                       lnb2[:, kc * 128:(kc + 1) * 128], idn[:],
                                       deps=([t_lnap2[par], t_ev[sl]]
                                             if u == 0 else None),
                                       inc=(u == 3))
                        t_ev[sl] = V.op(
                            V.e.tensor_copy,
                            lnT24[:, tb * 4:(tb + 1) * 4, tt, :],
                            plq[sl][:].rearrange("p (w c) -> p w c", c=128),
                            deps=[ttr, t_lnT2_free])
                    t_lnap2[par] = V.tok()
                groups = [(0, 6), (6, 6), (12, 4)]
                t_evb = [None] * 6
                t_evq = [None] * 16
                t_wqb4 = t_wqb + [None, None]
                for g0, gn in groups:
                    tmm = [None] * gn
                    for kc in range(16):
                        par = kc % 4
                        t_wqb4[par] = qWq4[par].dma(
                            wqb[:, par * 1024: par * 1024 + gn * 128],
                            wq_s[kc * 128:(kc + 1) * 128, g0 * 128:(g0 + gn) * 128],
                            deps=t_wqb4[par])
                        T.wait(t_wqb4[par])
                        for i in range(gn):
                            tmm[i] = T.op(
                                T.e.matmul, pq[i][:],
                                wqb[:, par * 1024 + i * 128: par * 1024 + (i + 1) * 128],
                                lnT2[:, kc * 512:(kc + 1) * 512],
                                start=(kc == 0), stop=(kc == 15),
                                deps=([t_ev[0], t_ev[1], t_evb[i], t_qfm_free]
                                      if kc == 0 else None),
                                inc=(kc == 15 or i == gn - 1))
                        t_wqb4[par] = tmm[gn - 1]
                    for i in range(gn):
                        t_evb[i] = SE.op(
                            SE.e.copy, q_raw[:, (g0 + i) * 512:(g0 + i + 1) * 512],
                            pq[i][:], deps=[tmm[i], t_qraw_free])
                        t_evq[g0 + i] = t_evb[i]
                t_wqb[0] = t_wqb4[0]
                t_wqb[1] = t_wqb4[1]
                barrier()
            with contextlib.ExitStack() as pa2:
                EA = pa2.enter_context
                pms2 = EA(nc.psum_tensor(f"pms2_{s}", [1, 512], F32))
                pB2 = EA(nc.psum_tensor(f"pB2_{s}", [128, 512], F32))
                t_msp = None
                t_pBe = None
                for hc in range(16):
                    t_sq = SE.op(SE.e.activation, q2f[:],
                                 q_raw[:, hc * 512:(hc + 1) * 512], AF.Square,
                                 deps=[t_msp])
                    tms = T.op(T.e.matmul, pms2[:], meanc[:], q2f[:], start=True,
                               stop=True, deps=[t_sq, t_msp], inc=True)
                    t_msp = tms
                    V.op(V.e.tensor_scalar, rq1[:], pms2[:], cpool[0:1, 2:3], 0.0,
                         ALU.add, ALU.bypass, deps=tms)
                    t_pms_e = V.tok()
                    SE.op(SE.e.activation, rq1[:], rq1[:], AF.Ln, deps=t_pms_e)
                    t_r = SE.op(SE.e.activation, rq1[:], rq1[:], AF.Exp,
                                scale=cpool[0:1, 6:7])
                    V.op(V.e.tensor_scalar, rq2[:], rq1[:], cpool[0:1, 4:5], 0.0,
                         ALU.mult, ALU.bypass, deps=t_r)
                    t_rc = V.op(V.e.scalar_tensor_tensor, rq2[:], rq1[:],
                                cpool[0:1, 5:6], rq2[:], ALU.mult, ALU.add)
                    tBm = T.op(T.e.matmul, pB2[:], qgr[:], rq2[:], start=True,
                               stop=True, deps=[t_rc, t_pBe], inc=True)
                    t_pBe = V.op(V.e.tensor_copy, Bs2[:], pB2[:], deps=tBm)
                    V.op(V.e.tensor_tensor, qfm[:, hc * 512:(hc + 1) * 512],
                         q_raw[:, hc * 512:(hc + 1) * 512], Bs2[:], ALU.mult)
                barrier()
            with contextlib.ExitStack() as pb:
                EB = pb.enter_context
                ps2 = [EB(nc.psum_tensor(f"ps2_{s}_{i}", [128, 1024], F32))
                       for i in range(2)]
                psv = [EB(nc.psum_tensor(f"psv_{s}_{i}", [128, 512], F32))
                       for i in range(2)]
                psB = [EB(nc.psum_tensor(f"psB_{s}_{i}", [128, 512], F32))
                       for i in range(2)]
                t_qbc = qBCs[0].dma(qbc[:],
                                    qidx[0:1, s * 512:(s + 1) * 512].broadcast_to([128, 512]))
                t_exp = [None, None]
                t_mask = [None, None]
                t_spv = [None, None]
                t_psv_free = [None, None]
                t_psB_free = [None, None]
                nkt = KTS[s]
                skipb = [0, 8, 16, 24][s]
                npr = nkt // 2
                for h in range(16):
                    hp = h % 2
                    kvh = h // 4
                    tqk_p = {}
                    t_sum_last = None
                    t_pv_last = None
                    for p in range(npr + 1):
                        if p < npr:
                            par = p % 2
                            tqk = None
                            for j in range(2):
                                kt = 2 * p + j
                                tqk = T.op(
                                    T.e.matmul, ps2[par][:, j * 512:(j + 1) * 512],
                                    kfm[:, kvh * S + kt * 128: kvh * S + (kt + 1) * 128],
                                    qfm[:, h * 512:(h + 1) * 512],
                                    start=True, stop=True,
                                    deps=[t_exp[par]] if j == 0 else None,
                                    inc=(j == 1))
                            tqk_p[p] = tqk
                        if p > 0:
                            q_ = p - 1
                            par = q_ % 2
                            t_exp[par] = SE.op(
                                SE.e.activation, Pm[:, par * 1024:(par + 1) * 1024],
                                ps2[par][:], AF.Exp, bias=cpool[:, 3:4],
                                deps=[tqk_p.pop(q_), t_mask[par], t_spv[par]])
                            tm = t_mask[par]
                            srcs = []
                            for j in range(2):
                                kt = 2 * q_ + j
                                sl = slice(par * 1024 + j * 512,
                                           par * 1024 + (j + 1) * 512)
                                if kt < skipb:
                                    srcs.append(Pm[:, sl])
                                else:
                                    tm = V.op(V.e.scalar_tensor_tensor, Pm2[:, sl],
                                              qbc[:], kidxT[:, kt:kt + 1], Pm[:, sl],
                                              ALU.is_ge, ALU.mult,
                                              deps=[t_exp[par], t_spv[par], t_qbc])
                                    srcs.append(Pm2[:, sl])
                            t_mask[par] = tm
                            for j in range(2):
                                kt = 2 * q_ + j
                                first = (kt == 0)
                                last = (kt == nkt - 1)
                                dps = [tm, t_exp[par]]
                                if first:
                                    dps.append(t_psB_free[hp])
                                t_sum_last = T.op(
                                    T.e.matmul, psB[hp][:], onesq[:], srcs[j],
                                    start=first, stop=last,
                                    deps=dps, inc=last)
                                t_pv_last = T.op(
                                    T.e.matmul, psv[hp][:],
                                    vtm[:, kt * 512 + kvh * 128: kt * 512 + (kvh + 1) * 128],
                                    srcs[j],
                                    start=first, stop=last,
                                    deps=[t_psv_free[hp]] if first else None,
                                    inc=(j == 1))
                            t_spv[par] = t_pv_last
                    t_psB_free[hp] = V.op(V.e.reciprocal, Bs2[:], psB[hp][:],
                                          deps=t_sum_last)
                    t_attn = V.op(V.e.tensor_tensor, attn[:, h * 512:(h + 1) * 512],
                                  psv[hp][:], Bs2[:], ALU.mult,
                                  deps=[t_pv_last, t_attn_free])
                    t_psv_free[hp] = t_attn
                barrier()
            with contextlib.ExitStack() as pc:
                EC = pc.enter_context
                po = [EC(nc.psum_tensor(f"po{s}_{i}", [128, 512], F32))
                      for i in range(2)]
                wobs = [lnT2, qfm]
                x1o = q_raw
                t_tt_init = [None] * 4
                t_x1w = [None] * 4
                t_po_free = [None, None]
                t_obmm = [None, None]
                for ob in range(4):
                    wob = wobs[ob % 2]
                    wdep = t_obmm[ob % 2]
                    if wdep is None:
                        wdep = t_lnT2_free if ob % 2 == 0 else t_qfm_free
                    t_wl = []
                    for hc in range(16):
                        t_wl.append(qWO[hc % 2].dma(
                            wob[:, hc * 512:(hc + 1) * 512],
                            wo_s[hc * 128:(hc + 1) * 128, ob * 512:(ob + 1) * 512],
                            deps=[wdep]))
                    for tt in range(4):
                        par = tt % 2
                        tmm = None
                        for hc in range(16):
                            tmm = T.op(T.e.matmul, po[par][:],
                                       attn[:, hc * 512 + tt * 128: hc * 512 + tt * 128 + 128],
                                       wob[:, hc * 512:(hc + 1) * 512],
                                       start=(hc == 0), stop=(hc == 15),
                                       deps=([t_wl[-1], t_wl[-2], t_po_free[par]]
                                             if hc == 0 else None),
                                       inc=(hc == 15))
                        t_obmm[ob % 2] = tmm
                        if ob == 0:
                            V.op(V.e.tensor_scalar, sAc[:],
                                 statsS[:, 2 * tt + 1:2 * tt + 2], 1.0, 0.0,
                                 ALU.add, ALU.bypass)
                            V.op(V.e.tensor_tensor, sBc[:],
                                 statsS[:, 2 * tt:2 * tt + 1],
                                 statsS[:, 2 * tt + 1:2 * tt + 2], ALU.mult)
                            V.op(V.e.tensor_scalar, sBc[:], sBc[:], -1.0, 0.0,
                                 ALU.mult, ALU.bypass)
                            par2 = tt % 2
                            xh = xb2[:, par2 * D:(par2 + 1) * D]
                            t_x = qX[par2].dma(
                                xh, x_own[(s * 4 + tt) * 128:(s * 4 + tt + 1) * 128, :],
                                deps=t_tt_init[par2])
                            t_tt_init[tt] = V.op(
                                V.e.tensor_scalar, x1o[:, tt * 2048:(tt + 1) * 2048],
                                xh, sAc[:], sBc[:], ALU.mult, ALU.add,
                                deps=[t_x, t_qraw_free])
                        t_add = V.op(
                            V.e.tensor_tensor,
                            x1o[:, tt * 2048 + ob * 512: tt * 2048 + (ob + 1) * 512],
                            x1o[:, tt * 2048 + ob * 512: tt * 2048 + (ob + 1) * 512],
                            po[par][:], ALU.add, deps=[tmm, t_tt_init[tt]])
                        t_po_free[par] = t_add
                        if ob == 3:
                            t_x1w[tt] = qX1.dma(
                                x1_s[(s * 4 + tt) * 128:(s * 4 + tt + 1) * 128, :],
                                x1o[:, tt * 2048:(tt + 1) * 2048], deps=t_add)
                t_qraw_free = Tok(sV, V.n)
                t_lnT2_free = Tok(sT, T.n)
                t_qfm_free = Tok(sT, T.n)
                t_attn_free = Tok(sT, T.n)
                barrier()

        pkv.close()
        if "t_x1" in tapd:
            for r in range(16):
                qC.dma(tapd["t_x1"][r * 128:(r + 1) * 128, :],
                       x1_s[r * 128:(r + 1) * 128, :])
        if upto == "p2":
            fin = qC.tok().resolve()
            nc.sync.wait_ge(fin.sem, fin.val)
            return nc

        gT = E(nc.sbuf_tensor("gT", [128, 64 * 512], BF16))
        h1T = E(nc.sbuf_tensor("h1T", [128, 16 * 512], BF16))
        wgt = E(nc.sbuf_tensor("wgt", [128, 3 * 8192], BF16))
        lnwbc = E(nc.sbuf_tensor("lnwbc", [128, D], BF16))
        lnbbc = E(nc.sbuf_tensor("lnbbc", [128, D], BF16))
        b2bc = E(nc.sbuf_tensor("b2bc", [128, D], BF16))
        b1bc = E(nc.sbuf_tensor("b1bc", [128, 2 * 512], F32))
        hbuf = E(nc.sbuf_tensor("hbuf", [128, D], F32))
        h1b = E(nc.sbuf_tensor("h1b", [128, D], BF16))
        xpb = E(nc.sbuf_tensor("xpb", [128, D], BF16))
        h2b = E(nc.sbuf_tensor("h2b", [128, D], BF16))
        x1b = E(nc.sbuf_tensor("x1b", [128, D], F32))
        g1 = E(nc.sbuf_tensor("g1", [128, D], BF16))
        gg = E(nc.sbuf_tensor("gg", [128, D], BF16))
        gq = E(nc.sbuf_tensor("gq", [128, 2 * D], BF16))
        stats2 = E(nc.sbuf_tensor("stats2", [128, 8], F32))
        smu3 = E(nc.sbuf_tensor("smu3", [128, 1], F32))
        srs3 = E(nc.sbuf_tensor("srs3", [128, 1], F32))
        stmp3 = E(nc.sbuf_tensor("stmp3", [128, 1], F32))
        bn63 = E(nc.sbuf_tensor("bn63", [128, 24], F32))
        bnm3 = E(nc.sbuf_tensor("bnm3", [128, 2], F32))
        qb3 = {}
        for nm in ("am", "ah", "sc", "iv"):
            qb3[nm] = E(nc.sbuf_tensor("q3_" + nm, [128, 64], F32))
        qb3["ei"] = E(nc.sbuf_tensor("q3_ei", [128, 64], I32))
        qb3["q0i"] = E(nc.sbuf_tensor("q3_q0i", [128, 2048], I32))
        qb3["yq"] = hbuf
        qb3["use_g"] = False
        qb3["cast_se"] = True
        qb3["emask"] = emask
        qX1R = [mkq("X1Ra"), mkq("X1Rb")]
        qWs = [qW[0], qWg_hold[0]]

        t_bc = []
        t_bc.append(qX1.dma(lnwbc[:], lnw.broadcast_to([128, D])))
        t_bc.append(qX1.dma(lnbbc[:], lnb.broadcast_to([128, D])))
        t_bc.append(qX1.dma(b2bc[:], b2r.broadcast_to([128, D])))
        V.wait(t_bc[1])
        V.wait(t_bc[2])

        for s in range(NSLOT):
            with contextlib.ExitStack() as pf1:
                EF = pf1.enter_context
                plf = [EF(nc.psum_tensor(f"plf{s}_{i}", [128, 512], BF16))
                       for i in range(2)]
                py1 = [EF(nc.psum_tensor(f"py1_{s}_{i}", [128, 512], F32))
                       for i in range(4)]
                t_x1b_free = None
                t_ev3 = [None, None]
                t_h1T_rd = None
                for tt in range(4):
                    par = tt % 2
                    t_x1 = qX1R[par].dma(
                        x1b[:], x1_s[(s * 4 + tt) * 128:(s * 4 + tt + 1) * 128, :],
                        deps=[t_x1b_free, qX1.tok()])
                    V.wait(t_x1)
                    lnstats(V, SE, bn63, bnm3, smu3, srs3, x1b[:], cpool, stmp3)
                    V.op(V.e.tensor_copy, stats2[:, 2 * tt:2 * tt + 1], smu3[:])
                    V.op(V.e.tensor_copy, stats2[:, 2 * tt + 1:2 * tt + 2], srs3[:],
                         deps=SE.tok())
                    t_xp = V.op(V.e.tensor_scalar, xpb[:], x1b[:], smu3[:], srs3[:],
                                ALU.subtract, ALU.mult)
                    lnstats(V, SE, bn63, bnm3, smu3, srs3, xpb[:], cpool, stmp3)
                    V.op(V.e.tensor_scalar, hbuf[:], xpb[:], smu3[:], srs3[:],
                         ALU.subtract, ALU.mult, deps=SE.tok())
                    V.op(V.e.tensor_tensor, hbuf[:], hbuf[:], lnwbc[:], ALU.mult)
                    V.op(V.e.tensor_tensor, h1b[:], hbuf[:], lnbbc[:], ALU.add)
                    qb3["xdep"] = [V.tok()]
                    qb3["se_war"] = V.tok()
                    tqh = mxq_chain(V, SE, G, h1b[:], 64, qb3, xpb[:])
                    qb3.pop("xdep", None)
                    t_x1b_free = tqh
                    if "t_h1q" in tapd and s == 0:
                        V.wait(qC.dma(tapd["t_h1q"][tt * 128:(tt + 1) * 128, :],
                                      xpb[:], deps=tqh))
                    h1T4 = h1T[:].rearrange("p (t o) -> p t o", o=512).rearrange(
                        "p t (w c) -> p t w c", c=128)
                    for tb in range(4):
                        sl = tb % 2
                        ttr = None
                        for u in range(4):
                            kc = tb * 4 + u
                            ttr = T.op(T.e.transpose,
                                       plf[sl][:, u * 128:(u + 1) * 128],
                                       xpb[:, kc * 128:(kc + 1) * 128], idn[:],
                                       deps=([V.tok(), t_ev3[sl], t_h1T_rd]
                                             if u == 0 else None),
                                       inc=(u == 3))
                        t_ev3[sl] = V.op(
                            V.e.tensor_copy,
                            h1T4[:, tb * 4:(tb + 1) * 4, tt, :],
                            plf[sl][:].rearrange("p (w c) -> p w c", c=128),
                            deps=ttr)
                t_w1 = [None, None]
                t_g1v = [None, None]
                t_b1 = [None, None]
                t_py1_free = [None] * 4
                pend = []
                t_gq_free = None
                gT4 = gT[:].rearrange("p (t o) -> p t o", o=512).rearrange(
                    "p t (w c) -> p t w c", c=128)

                def do_pend(tok_dep=None):
                    ob_, tt_, tq_ = pend.pop(0)
                    gqh = gq[:, (ob_ % 2) * D:(ob_ % 2 + 1) * D]
                    sl = (ob_ * 4 + tt_) % 2
                    ttr = None
                    for j in range(4):
                        ttr = T.op(T.e.transpose,
                                   plf[sl][:, j * 128:(j + 1) * 128],
                                   gqh[:, tt_ * 512 + j * 128: tt_ * 512 + (j + 1) * 128],
                                   idn[:],
                                   deps=([tq_, t_ev3[sl]] if j == 0 else None),
                                   inc=(j == 3))
                    t_ev3[sl] = V.op(
                        V.e.tensor_copy,
                        gT4[:, ob_ * 4:(ob_ + 1) * 4, tt_, :],
                        plf[sl][:].rearrange("p (w c) -> p w c", c=128),
                        deps=ttr)

                for ob in range(16):
                    par = ob % 2
                    t_w1l = qWs[par].dma(
                        wgt[:, par * 8192:(par + 1) * 8192].rearrange(
                            "p (i j) -> p i j", j=512),
                        w1tile_s[ob * 2048:(ob + 1) * 2048, :].rearrange(
                            "(i p) j -> p i j", p=128),
                        deps=t_w1[par])
                    t_b1[par] = qWO[par].dma(
                        b1bc[:, par * 512:(par + 1) * 512],
                        b1r[0:1, ob * 512:(ob + 1) * 512].broadcast_to([128, 512]),
                        deps=t_g1v[par])
                    tmm = None
                    for tt in range(4):
                        pp = tt
                        dps0 = [t_w1l, t_py1_free[pp]]
                        if ob == 0:
                            dps0 += [t_ev3[0], t_ev3[1]]
                        for kc in range(16):
                            tmm = T.op(
                                T.e.matmul, py1[pp][:],
                                h1T[:, kc * 512 + tt * 128: kc * 512 + tt * 128 + 128],
                                wgt[:, par * 8192 + kc * 512: par * 8192 + (kc + 1) * 512],
                                start=(kc == 0), stop=(kc == 15),
                                deps=(dps0 if kc == 0 else None),
                                inc=(kc == 15))
                        if len(pend) > 4:
                            do_pend()
                        tg1 = V.op(V.e.scalar_tensor_tensor,
                                   g1[:, tt * 512:(tt + 1) * 512], py1[pp][:], 0.0,
                                   b1bc[:, par * 512:(par + 1) * 512],
                                   ALU.bypass, ALU.add, deps=[tmm, t_b1[par]])
                        t_py1_free[pp] = tg1
                        t_g1v[par] = tg1
                    tgg = SE.op(SE.e.activation, gg[:], g1[:], AF.Gelu_apprx_tanh,
                                deps=t_g1v[par])
                    qb3["xdep"] = [tgg, V.tok()]
                    qb3["se_war"] = V.tok()
                    gqh = gq[:, (ob % 2) * D:(ob % 2 + 1) * D]
                    tq = mxq_chain(V, SE, G, gg[:], 64, qb3, gqh)
                    qb3.pop("xdep", None)
                    if "t_gq" in tapd and s == 0 and ob == 0:
                        tk = None
                        for tt_ in range(4):
                            tk = qC.dma(tapd["t_gq"][tt_ * 128:(tt_ + 1) * 128, 0:512],
                                        gqh[:, tt_ * 512:(tt_ + 1) * 512], deps=tq)
                        V.wait(tk)
                    for tt in range(4):
                        pend.append((ob, tt, tq))
                    t_w1[par] = tmm
                while pend:
                    do_pend()
                barrier()
            with contextlib.ExitStack() as pf2:
                EF = pf2.enter_context
                py2 = [EF(nc.psum_tensor(f"py2_{s}_{i}", [128, 2048], F32))
                       for i in range(2)]
                qW4 = [qWs[0], qWs[1], qWO[0]]
                t_w2l = [None] * 3
                t_w2g = [None] * 3
                t_ep = [None, None]
                t_x1b_free2 = None
                for htt in range(2):
                    tmm = None
                    for grp in range(16):
                        par = grp % 3
                        t_gate = t_w2g[par]
                        t_w2l[par] = qW4[par].dma(
                            wgt[:, par * 8192:(par + 1) * 8192].rearrange(
                                "p (i j) -> p i j", j=2048),
                            w2t_s[grp * 512:(grp + 1) * 512, :].rearrange(
                                "(i p) j -> p i j", p=128),
                            deps=t_gate)
                        for tt in (2 * htt, 2 * htt + 1):
                            pp = tt % 2
                            for i in range(4):
                                kc2 = grp * 4 + i
                                for o2b in range(4):
                                    grp_last = (i == 3 and o2b == 3
                                                and tt == 2 * htt + 1)
                                    dps = None
                                    if i == 0 and o2b == 0:
                                        dps = [t_w2l[par]]
                                        if grp == 0:
                                            dps.append(t_ep[pp])
                                    tmm = T.op(
                                        T.e.matmul,
                                        py2[pp][:, o2b * 512:(o2b + 1) * 512],
                                        gT[:, kc2 * 512 + tt * 128: kc2 * 512 + tt * 128 + 128],
                                        wgt[:, par * 8192 + i * 2048 + o2b * 512:
                                            par * 8192 + i * 2048 + (o2b + 1) * 512],
                                        start=(grp == 0 and i == 0),
                                        stop=(grp == 15 and i == 3),
                                        deps=dps,
                                        inc=grp_last)
                        t_w2g[par] = tmm
                    for tt in (2 * htt, 2 * htt + 1):
                        pp = tt % 2
                        t_h2 = V.op(V.e.scalar_tensor_tensor, h2b[:], py2[pp][:],
                                    0.0, b2bc[:], ALU.bypass, ALU.add, deps=tmm)
                        t_ep[pp] = t_h2
                        par = tt % 2
                        t_x1 = qX1R[par].dma(
                            x1b[:], x1_s[(s * 4 + tt) * 128:(s * 4 + tt + 1) * 128, :],
                            deps=t_x1b_free2)
                        V.op(V.e.tensor_scalar, xpb[:], x1b[:],
                             stats2[:, 2 * tt:2 * tt + 1],
                             stats2[:, 2 * tt + 1:2 * tt + 2],
                             ALU.subtract, ALU.mult, deps=t_x1)
                        V.op(V.e.tensor_tensor, h2b[:], xpb[:], h2b[:], ALU.add)
                        t_out = V.op(V.e.tensor_tensor, x1b[:], x1b[:], h2b[:],
                                     ALU.add)
                        t_x1b_free2 = qX1.dma(
                            out_own[(s * 4 + tt) * 128:(s * 4 + tt + 1) * 128, :],
                            x1b[:], deps=t_out)
                barrier()

        fin = qX1.tok().resolve()
        nc.sync.wait_ge(fin.sem, fin.val)
    return nc


def _host_consts():
    import ml_dtypes as _md
    BF = _md.bfloat16
    c = {}
    c["onescol"] = np.ones((128, 1), BF)
    c["onesq_in"] = np.ones((128, 128), BF)
    c["onesrowf"] = np.ones((1, 128), np.float32)
    c["meancol"] = np.full((128, 1), 1.0 / 128, np.float32)
    c["idn_in"] = np.eye(128).astype(BF)
    c["emask_in"] = np.full((128, 1), 0x7F800000, np.int32)
    eps5 = np.float32(1e-5)
    cs = np.float32(C_SCALE)
    cp = np.zeros((128, 8), np.float32)
    cp[:, 0] = eps5
    cp[:, 1] = np.float32(1e-5 - np.float64(eps5))
    cp[:, 2] = np.float32(1e-8)
    cp[:, 3] = -12.0
    cp[:, 4] = cs
    cp[:, 5] = np.float32(np.float64(C_SCALE) - np.float64(cs))
    cp[:, 6] = -0.5
    c["cpool_in"] = cp
    return c


_BLOCKS = {0: [0, 3, 4, 7], 1: [1, 2, 5, 6]}


def core_in_map(core, x, Wq, Wk, Wv, Wo, W1, W2, qg, kg, lnw, lnb, b1, b2):
    b, half = core // 2, core % 2
    blocks = _BLOCKS[half]
    xb = np.ascontiguousarray(np.asarray(x[b], np.float32))
    d = dict(_host_consts())
    d["x_seq"] = xb
    d["x_own"] = np.ascontiguousarray(
        np.concatenate([xb[blk * 512:(blk + 1) * 512] for blk in blocks], 0))
    d["qidx"] = np.concatenate(
        [np.arange(blk * 512, (blk + 1) * 512, dtype=np.float32)
         for blk in blocks]).reshape(1, -1)
    d["Wq"] = Wq
    d["Wk"] = Wk
    d["Wv"] = Wv
    d["Wo"] = Wo
    d["W1"] = W1
    d["W2"] = W2
    d["qg"] = qg.reshape(1, 128)
    d["kg"] = kg.reshape(1, 128)
    d["lnw"] = lnw.reshape(1, -1)
    d["lnb"] = lnb.reshape(1, -1)
    d["b1r"] = b1.reshape(1, -1)
    d["b2r"] = b2.reshape(1, -1)
    return d


def kernel(L0_attn_x, L0_attn_Wq, L0_attn_Wk, L0_attn_Wv, L0_attn_Wo,
           L0_attn_q_gamma, L0_attn_k_gamma, L0_ffn_ln_w, L0_ffn_ln_b,
           L0_ffn_W1, L0_ffn_b1, L0_ffn_W2, L0_ffn_b2):
    ws = [np.ascontiguousarray(np.asarray(w, np.float32)) for w in
          (L0_attn_Wq, L0_attn_Wk, L0_attn_Wv, L0_attn_Wo, L0_ffn_W1, L0_ffn_W2,
           L0_attn_q_gamma, L0_attn_k_gamma, L0_ffn_ln_w, L0_ffn_ln_b,
           L0_ffn_b1, L0_ffn_b2)]
    nc = build(upto="p3", taps=())
    in_maps = [core_in_map(c, L0_attn_x, ws[0], ws[1], ws[2], ws[3], ws[4],
                           ws[5], ws[6], ws[7], ws[8], ws[9], ws[10], ws[11])
               for c in range(8)]
    res = run_bass_kernel_spmd(nc, in_maps, core_ids=list(range(8)))
    out = np.empty((NB, S, D), np.float32)
    for core in range(8):
        blocks = _BLOCKS[core % 2]
        oo = np.asarray(res.results[core]["out_own"])
        for si, blk in enumerate(blocks):
            out[core // 2, blk * 512:(blk + 1) * 512] = oo[si * 512:(si + 1) * 512]
    return out
